# version 05
# baseline (speedup 1.0000x reference)
import contextlib
import numpy as np
import ml_dtypes
import concourse.bass as bass
import concourse.mybir as mybir
from concourse.bass_utils import run_bass_kernel_spmd

F32 = mybir.dt.float32
BF16 = mybir.dt.bfloat16
I32 = mybir.dt.int32
AF = mybir.ActivationFunctionType
ALU = mybir.AluOpType
AX = mybir.AxisListType.X

NCORES = 8
BC, L, D, DI, NST, DTRK, IFF = 4, 2048, 1024, 2048, 16, 64, 2816
TCH = 512
NQ = L // TCH
NDC = DI // 128
NHC = D // 128
NIC = IFF // 128
BAND = 4
NBAND = NDC // BAND
ISL_W = [512, 512, 512, 512, 512, 256]

MSK_ABS = 0x7FFFFFFF
MSK_SGN = 0x80000000 - (1 << 32)
MSK_RND = 0xFFC00000 - (1 << 32)
B_ONE = 0x3F800000
B_SIX = 0x40C00000

ENGS = ("q", "v", "s", "t", "g")


class Plan:

    def __init__(self):
        self.items = []
        self.cnt = {}

    def sem(self, name):
        self.cnt[name] = 0

    def op(self, eng, fn=None, wait=(), inc=()):
        inc = tuple(inc)
        for s, by in inc:
            self.cnt[s] += by
        self.items.append((eng, fn, tuple(wait), inc))

    def val(self, name):
        return self.cnt[name]

    def waits_all(self):
        return [(s, v) for s, v in self.cnt.items()]


def build_nc(debug=False, nchunks=BC * NQ, stages=("A", "B", "S2", "S3", "S4"),
             do_startup=("small", "win", "wout", "ffn")):
    nc = bass.Bass("TRN2", target_bir_lowering=False, debug=False)
    X = nc.dram_tensor("x_in", [BC, L, D], F32, kind="ExternalInput").ap()
    LNW = nc.dram_tensor("lnw", [128, NHC], F32, kind="ExternalInput").ap()
    LNB = nc.dram_tensor("lnb", [128, NHC], F32, kind="ExternalInput").ap()
    ALOG = nc.dram_tensor("alog", [128, NDC * NST], F32, kind="ExternalInput").ap()
    DPAR = nc.dram_tensor("dpar", [128, NDC], F32, kind="ExternalInput").ap()
    CONVW = nc.dram_tensor("convw", [128, NDC * 4], F32, kind="ExternalInput").ap()
    CONVB = nc.dram_tensor("convb", [128, NDC], F32, kind="ExternalInput").ap()
    BDT = nc.dram_tensor("bdt", [128, NDC], F32, kind="ExternalInput").ap()
    WIN = nc.dram_tensor("w_in", [D, 2 * DI], F32, kind="ExternalInput").ap()
    WXP = nc.dram_tensor("w_xp", [DI, 128], F32, kind="ExternalInput").ap()
    WDT = nc.dram_tensor("w_dt", [DTRK, DI], F32, kind="ExternalInput").ap()
    WOUT = nc.dram_tensor("w_out", [DI, D], F32, kind="ExternalInput").ap()
    WG = nc.dram_tensor("w_g", [IFF, D], F32, kind="ExternalInput").ap()
    WU = nc.dram_tensor("w_u", [IFF, D], F32, kind="ExternalInput").ap()
    WDN = nc.dram_tensor("w_d", [D, IFF], F32, kind="ExternalInput").ap()
    IDF = nc.dram_tensor("idf", [128, 128], F32, kind="ExternalInput").ap()
    IDB = nc.dram_tensor("idb", [128, 128], BF16, kind="ExternalInput").ap()
    OUT = nc.dram_tensor("out", [BC, D, L], F32, kind="ExternalOutput").ap()

    WINS = nc.dram_tensor("wins", [D, 2 * DI], BF16, kind="Internal").ap()
    WOUTS = nc.dram_tensor("wouts", [DI, D], BF16, kind="Internal").ap()
    WGQT = nc.dram_tensor("wgqt", [D, IFF], BF16, kind="Internal").ap()
    WUQT = nc.dram_tensor("wuqt", [D, IFF], BF16, kind="Internal").ap()
    WDQT = nc.dram_tensor("wdqt", [IFF, D], BF16, kind="Internal").ap()
    LNBW = nc.dram_tensor("lnbw_d", [1, 2 * DI], F32, kind="Internal").ap()
    BCS = nc.dram_tensor("bcs", [2 * NST, TCH], BF16, kind="Internal").ap()

    dbg_outs = {}

    es = contextlib.ExitStack()

    def sbt(name, shape, dt_):
        return es.enter_context(nc.sbuf_tensor(name, shape, dt_))

    def pst(name, shape, dt_):
        return es.enter_context(nc.psum_tensor(name, shape, dt_))

    s_lnw = sbt("s_lnw", [128, NHC], F32)
    s_lnb = sbt("s_lnb", [128, NHC], F32)
    s_A = sbt("s_A", [128, NDC * NST], F32)
    s_D = sbt("s_D", [128, NDC], F32)
    s_cw = sbt("s_cw", [128, NDC * 4], F32)
    s_cb = sbt("s_cb", [128, NDC], F32)
    s_bdt = sbt("s_bdt", [128, NDC], F32)
    s_lnbw = sbt("s_lnbw", [128, 2 * DI // 128], F32)
    s_idf = sbt("s_idf", [128, 128], F32)
    s_idb = sbt("s_idb", [128, 128], BF16)
    s_carry = sbt("s_carry", [128, BC * NDC * NST], F32)
    s_ctail = sbt("s_ctail", [128, BC * NDC * 3], F32)
    s_wdt = sbt("s_wdt", [DTRK, DI], BF16)
    s_wxp = sbt("s_wxp", [128, NDC * 128], BF16)
    s_lnrow = sbt("s_lnrow", [1, 2 * DI], F32)
    s_kc = sbt("s_kc", [128, 4], F32)
    s_lnbb = sbt("s_lnbb", [128, NHC], BF16)
    s_xt = sbt("s_xt", [128, D], F32)
    s_stat = sbt("s_stat", [128, 16], F32)
    s_xn = sbt("s_xn", [128, D], F32)
    s_xnb = sbt("s_xnb", [128, D], BF16)
    s_xmb = sbt("s_xmb", [128, D], BF16)
    s_xq = sbt("s_xq", [128, D], BF16)
    w1 = sbt("wk1", [128, D], F32)
    w2 = sbt("wk2", [128, D], F32)
    w3 = sbt("wk3", [128, D], F32)
    q_bm = sbt("q_bm", [128, IFF // 16], F32)
    q_sc = sbt("q_sc", [128, IFF // 16], F32)
    q_rs = sbt("q_rs", [128, IFF // 16], F32)
    s_ar1 = sbt("s_ar1", [128, 2 * NHC * TCH], BF16)
    s_xnfm = sbt("s_xnfm", [128, NHC * TCH], BF16)
    s_u = sbt("s_u", [128, NDC * TCH], BF16)
    s_silz = sbt("s_silz", [128, NDC * TCH], BF16)
    s_xpad = sbt("s_xpad", [128, TCH + 3], F32)
    s_wf = sbt("s_wf", [128, 8 * 128], BF16)
    s_wg = sbt("s_wg", [128, 8 * 512], BF16)
    s_wu = sbt("s_wu", [128, 8 * 512], BF16)
    s_sg = sbt("s_sg", [128, 512], F32)
    s_h = sbt("s_h", [128, 512], F32)
    s_hq = sbt("s_hq", [128, 512], BF16)
    s_hqT = sbt("s_hqT", [128, NIC * TCH], BF16)
    s_asm = sbt("s_asm", [128, NHC * TCH], BF16)
    s_wdq = sbt("s_wdq", [128, NIC * 128], BF16)
    s_dtr = sbt("s_dtr", [DTRK, TCH], BF16)
    s_Brow = sbt("s_Brow", [16, TCH], BF16)
    s_Crow = sbt("s_Crow", [16, TCH], BF16)
    s_dt = sbt("s_dt", [128, BAND * TCH], F32)
    s_w = sbt("s_w", [128, BAND * TCH], BF16)
    s_dA = sbt("s_dA", [128, 2 * TCH], F32)
    s_P = sbt("s_P", [128, 2 * TCH], BF16)
    s_Bbc = sbt("s_Bbc", [128, 2 * TCH], BF16)
    s_Cbc = sbt("s_Cbc", [128, 2 * TCH], BF16)
    s_hs = sbt("s_hs", [128, BAND * TCH], BF16)
    s_Ch = sbt("s_Ch", [128, 2 * TCH], BF16)
    s_xr = s_wg.ap().bitcast(F32)

    pA = pst("pA", [128, 512], F32)
    pB = pst("pB", [128, 512], F32)
    pC = pst("pC", [128, 512], F32)
    pY = [pst(f"pY{i}", [128, 512], F32) for i in range(BAND)]
    pT = pst("pT", [128, 1024], BF16)

    stage_f = s_u.ap().bitcast(F32)
    stage_b = s_silz.ap()

    for _val, _col in ((1e-5, 0), (-0.5, 1), (1.0, 2), (-1.0, 3)):
        try:
            nc.const_aps.register(F32, _val, s_kc[:, _col:_col + 1])
        except Exception:
            pass

    P = Plan()
    NQS = 4
    qs_names = [f"qs{i}" for i in range(NQS)]
    eg_names = [f"eg{i}" for i in range(2)]
    for s in ["ev", "es", "et"] + qs_names + eg_names:
        P.sem(s)
    sem_handles = {}

    def qs_of(ci):
        return qs_names[ci % NQS]

    def eg_of(ci):
        return eg_names[ci % 2]

    QS = "qs0"
    EG = "eg0"

    def dbg(name, src_ap, shape, dt_, when_ev=None):
        if not debug or name in dbg_outs:
            return
        t = nc.dram_tensor("dbg_" + name, shape, dt_, kind="ExternalOutput").ap()
        dbg_outs[name] = t
        wl = [("ev", P.val("ev")), ("es", P.val("es")), ("et", P.val("et"))]
        P.op("g", lambda g, t=t, src=src_ap: g.dma_start(out=t, in_=src),
             wait=wl, inc=[(EG, 16)])

    def barrier():
        w = P.waits_all()
        for e in ENGS:
            P.op(e, None, wait=w, inc=[("ev" if e == "v" else "es" if e == "s"
                                        else "et" if e == "t" else qs_of(0) if e == "q"
                                        else eg_names[0], 1 if e in ("v", "s", "t") else 16)])

    def quant(src_f32, dst_bf, ncols, extra_wait=()):
        nb = ncols // 16

        def v1(v, src=src_f32, nb=nb):
            v.tensor_reduce(out=q_bm[:, :nb],
                            in_=src.rearrange("p (s k) -> p s k", k=16),
                            axis=AX, op=ALU.max, apply_absolute_value=True)
            v.engine_nop()
            v.engine_nop()
            v.tensor_scalar(out=q_sc[:, :nb], in0=q_bm[:, :nb],
                            scalar1=1.0 / 6.0, scalar2=1e-12,
                            op0=ALU.mult, op1=ALU.max)
            v.engine_nop()
            v.engine_nop()
            return v.reciprocal(out=q_rs[:, :nb], in_=q_sc[:, :nb])
        P.op("v", v1, wait=extra_wait, inc=[("ev", 1)])

        def v2(v, src=src_f32, dst=dst_bf, ncols=ncols, nb=nb):
            v.tensor_scalar(out=w1[:, :ncols].bitcast(I32), in0=src.bitcast(I32),
                            scalar1=MSK_ABS, scalar2=None, op0=ALU.bitwise_and)
            rs_b = bass.AP(q_rs, 0, [[IFF // 16, 128], [1, nb], [0, 16]])
            v.tensor_tensor(out=w3[:, :ncols].rearrange("p (s k) -> p s k", k=16),
                            in0=w1[:, :ncols].rearrange("p (s k) -> p s k", k=16),
                            in1=rs_b, op=ALU.mult)
            v.tensor_scalar(out=w1[:, :ncols], in0=w3[:, :ncols], scalar1=-0.25,
                            scalar2=1e38, op0=ALU.add, op1=ALU.mult)
            v.tensor_scalar(out=w1[:, :ncols], in0=w1[:, :ncols], scalar1=0.0,
                            scalar2=0.5, op0=ALU.max, op1=ALU.min)
            v.tensor_scalar(out=w2[:, :ncols], in0=w3[:, :ncols], scalar1=-0.75,
                            scalar2=1e38, op0=ALU.add, op1=ALU.mult)
            v.tensor_scalar(out=w3[:, :ncols].bitcast(I32), in0=w3[:, :ncols].bitcast(I32),
                            scalar1=B_ONE, scalar2=B_SIX, op0=ALU.max, op1=ALU.min)
            v.tensor_scalar(out=w3[:, :ncols].bitcast(I32), in0=w3[:, :ncols].bitcast(I32),
                            scalar1=0x200000, scalar2=None, op0=ALU.add)
            v.tensor_scalar(out=w3[:, :ncols].bitcast(I32), in0=w3[:, :ncols].bitcast(I32),
                            scalar1=MSK_RND, scalar2=None, op0=ALU.bitwise_and)
            v.tensor_tensor(out=w2[:, :ncols], in0=w2[:, :ncols],
                            in1=w3[:, :ncols], op=ALU.min)
            v.tensor_tensor(out=w2[:, :ncols], in0=w2[:, :ncols],
                            in1=w1[:, :ncols], op=ALU.max)
            v.tensor_scalar(out=w1[:, :ncols].bitcast(I32), in0=src.bitcast(I32),
                            scalar1=MSK_SGN, scalar2=None, op0=ALU.bitwise_and)
            v.tensor_tensor(out=w2[:, :ncols].bitcast(I32), in0=w2[:, :ncols].bitcast(I32),
                            in1=w1[:, :ncols].bitcast(I32), op=ALU.bitwise_or)
            sc_b = bass.AP(q_sc, 0, [[IFF // 16, 128], [1, nb], [0, 16]])
            return v.tensor_tensor(out=dst.rearrange("p (s k) -> p s k", k=16),
                                   in0=w2[:, :ncols].rearrange("p (s k) -> p s k", k=16),
                                   in1=sc_b, op=ALU.mult)
        P.op("v", v2, wait=(), inc=[("ev", 1)])

    smalls = [(s_lnw, LNW), (s_lnb, LNB), (s_A, ALOG), (s_D, DPAR),
              (s_cw, CONVW), (s_cb, CONVB), (s_bdt, BDT), (s_idf, IDF),
              (s_idb, IDB)]
    for dst, src in smalls:
        P.op("q", lambda q, d=dst, s=src: q.dma_start(out=d[:], in_=s),
             inc=[(QS, 16)])
    P.op("s", lambda s: s.activation(out=s_A[:], in_=s_A[:], func=AF.Exp),
         wait=[(QS, P.val(QS))], inc=[("es", 1)])
    P.op("v", lambda v: v.tensor_scalar(out=s_A[:], in0=s_A[:], scalar1=-1.0,
                                        scalar2=None, op0=ALU.mult),
         wait=[("es", P.val("es"))], inc=[("ev", 1)])
    P.op("v", lambda v: v.tensor_copy(out=s_lnbb[:], in_=s_lnb[:]),
         wait=[("qs0", 16 * 2)], inc=[("ev", 1)])
    P.op("v", lambda v: v.memset(s_kc[:, 0:1], 1e-5))
    P.op("v", lambda v: v.memset(s_kc[:, 1:2], -0.5))
    P.op("v", lambda v: v.memset(s_kc[:, 2:3], 1.0))
    P.op("v", lambda v: v.memset(s_kc[:, 3:4], -1.0))
    P.op("v", lambda v: v.memset(s_carry[:], 0.0))
    P.op("v", lambda v: v.memset(s_ctail[:], 0.0))
    P.op("v", lambda v: v.memset(s_lnrow[:], 0.0), inc=[("ev", 1)])

    P.op("q", lambda q: q.dma_start(out=stage_f[0:DTRK, :DI], in_=WDT),
         wait=[("ev", P.val("ev"))], inc=[(QS, 16)])
    P.op("v", lambda v: v.tensor_copy(out=s_wdt[:], in_=stage_f[0:DTRK, :DI]),
         wait=[(QS, P.val(QS))], inc=[("ev", 1)])
    wxp_src = bass.AP(WXP.tensor, 0, [[128, 128], [128 * 128, NDC], [1, 128]])
    P.op("q", lambda q: q.dma_start(out=stage_f[:, DI:DI + NDC * 128], in_=wxp_src),
         inc=[(QS, 16)])
    P.op("v", lambda v: v.tensor_copy(out=s_wxp[:], in_=stage_f[:, DI:DI + NDC * 128]),
         wait=[(QS, P.val(QS))], inc=[("ev", 1)])

    for rc in range(NHC):
        P.op("q", lambda q, rc=rc: q.dma_start(out=stage_f[:, :2 * DI],
                                               in_=WIN[rc * 128:(rc + 1) * 128, :]),
             wait=[("ev", P.val("ev"))], inc=[(QS, 16)])
        P.op("v", lambda v: v.tensor_copy(out=stage_b[:, :2 * DI],
                                          in_=stage_f[:, :2 * DI]),
             wait=[(QS, P.val(QS)), (EG, P.val(EG))], inc=[("ev", 1)])
        for fs in range(8):
            P.op("t", lambda t, rc=rc, fs=fs: t.matmul(
                out=pA[0:1, :], lhsT=s_lnbb[:, rc:rc + 1],
                rhs=stage_b[:, fs * 512:(fs + 1) * 512], start=True, stop=True),
                wait=[("ev", P.val("ev"))], inc=[("et", 1)])
            P.op("v", lambda v, fs=fs: v.tensor_tensor(
                out=s_lnrow[0:1, fs * 512:(fs + 1) * 512],
                in0=s_lnrow[0:1, fs * 512:(fs + 1) * 512], in1=pA[0:1, :],
                op=ALU.add), wait=[("et", P.val("et"))], inc=[("ev", 1)])
        P.op("v", lambda v, rc=rc: v.tensor_scalar(
            out=stage_b[:, :2 * DI], in0=stage_f[:, :2 * DI],
            scalar1=s_lnw[:, rc:rc + 1], scalar2=None, op0=ALU.mult),
            wait=[("et", P.val("et"))], inc=[("ev", 1)])
        P.op("g", lambda g, rc=rc: g.dma_start(out=WINS[rc * 128:(rc + 1) * 128, :],
                                               in_=stage_b[:, :2 * DI]),
             wait=[("ev", P.val("ev"))], inc=[(EG, 16)])
    P.op("g", lambda g: g.dma_start(out=LNBW, in_=s_lnrow[:]),
         wait=[("ev", P.val("ev"))], inc=[(EG, 16)])
    lnbw_src = bass.AP(LNBW.tensor, 0, [[1, 128], [128, 2 * DI // 128]])
    P.op("g", lambda g: g.dma_start(out=s_lnbw[:], in_=lnbw_src,
                                    allow_slow_non_contiguous=True),
         wait=[(EG, P.val(EG))], inc=[(EG, 16)])

    for rc in range(NDC):
        P.op("q", lambda q, rc=rc: q.dma_start(out=stage_f[:, :D],
                                               in_=WOUT[rc * 128:(rc + 1) * 128, :]),
             wait=[(EG, P.val(EG))], inc=[(QS, 16)])
        P.op("v", lambda v: v.tensor_copy(out=stage_b[:, :D], in_=stage_f[:, :D]),
             wait=[(QS, P.val(QS))], inc=[("ev", 1)])
        P.op("g", lambda g, rc=rc: g.dma_start(out=WOUTS[rc * 128:(rc + 1) * 128, :],
                                               in_=stage_b[:, :D]),
             wait=[("ev", P.val("ev"))], inc=[(EG, 16)])

    for WSRC, WDST in ((WG, WGQT), (WU, WUQT)):
        for ic in range(NIC):
            P.op("q", lambda q, W=WSRC, ic=ic: q.dma_start(
                out=stage_f[:, :D], in_=W[ic * 128:(ic + 1) * 128, :]),
                wait=[(EG, P.val(EG))], inc=[(QS, 16)])
            quant(stage_f[:, :D], stage_b[:, :D], D,
                  extra_wait=[(QS, P.val(QS)), (EG, P.val(EG))])
            for hc in range(NHC):
                P.op("t", lambda t, hc=hc: t.matmul(
                    out=pT[:, hc * 128:(hc + 1) * 128],
                    lhsT=stage_b[:, hc * 128:(hc + 1) * 128], rhs=s_idb[:],
                    is_transpose=True, start=True, stop=True),
                    wait=([("ev", P.val("ev"))] if hc == 0 else ()),
                    inc=([("et", 1)] if hc == NHC - 1 else ()))
            P.op("v", lambda v: v.tensor_copy(out=stage_b[:, D:2 * D], in_=pT[:, :D]),
                 wait=[("et", P.val("et"))], inc=[("ev", 1)])
            for hc in range(NHC):
                P.op("g", lambda g, W=WDST, ic=ic, hc=hc: g.dma_start(
                    out=W[hc * 128:(hc + 1) * 128, ic * 128:(ic + 1) * 128],
                    in_=stage_b[:, D + hc * 128:D + (hc + 1) * 128]),
                    wait=[("ev", P.val("ev"))], inc=[(EG, 16)])

    for rc in range(NHC):
        P.op("q", lambda q, rc=rc: q.dma_start(
            out=stage_f[:, :IFF], in_=WDN[rc * 128:(rc + 1) * 128, :]),
            wait=[(EG, P.val(EG))], inc=[(QS, 16)])
        quant(stage_f[:, :1024], stage_b[:, :1024], 1024,
              extra_wait=[(QS, P.val(QS)), (EG, P.val(EG))])
        quant(stage_f[:, 1024:2048], stage_b[:, 1024:2048], 1024)
        quant(stage_f[:, 2048:IFF], stage_b[:, 2048:IFF], 768)
        for grp in range(3):
            blocks = range(grp * 8, min((grp + 1) * 8, NIC))
            for j, ic in enumerate(blocks):
                P.op("t", lambda t, j=j, ic=ic: t.matmul(
                    out=pT[:, j * 128:(j + 1) * 128],
                    lhsT=stage_b[:, ic * 128:(ic + 1) * 128], rhs=s_idb[:],
                    is_transpose=True, start=True, stop=True),
                    wait=([("ev", P.val("ev"))] if j == 0 else ()),
                    inc=([("et", 1)] if ic == list(blocks)[-1] else ()))
            nb = len(list(blocks))
            P.op("v", lambda v, grp=grp, nb=nb: v.tensor_copy(
                out=stage_b[:, IFF + grp * 1024:IFF + grp * 1024 + nb * 128],
                in_=pT[:, :nb * 128]),
                wait=[("et", P.val("et"))], inc=[("ev", 1)])
            for j, ic in enumerate(blocks):
                P.op("g", lambda g, rc=rc, ic=ic, grp=grp, j=j: g.dma_start(
                    out=WDQT[ic * 128:(ic + 1) * 128, rc * 128:(rc + 1) * 128],
                    in_=stage_b[:, IFF + grp * 1024 + j * 128:IFF + grp * 1024 + (j + 1) * 128]),
                    wait=[("ev", P.val("ev"))], inc=[(EG, 16)])

    barrier()

    for ci in range(BC * NQ):
        b, qq = divmod(ci, NQ)
        QSc, EGc = qs_of(ci), eg_of(ci)
        tok0 = qq * TCH

        prev_tile_done = {"ev": P.val("ev"), "es": P.val("es")}
        for tt in range(4):
            P.op("q", lambda q, b=b, r0=tok0 + tt * 128: q.dma_start(
                out=s_xt[:], in_=X[b, r0:r0 + 128, :]),
                wait=[("ev", prev_tile_done["ev"]), ("es", prev_tile_done["es"])],
                inc=[(QSc, 16)])
            P.op("v", lambda v: v.reduce_sum(out=s_stat[:, 0:1], in_=s_xt[:], axis=AX),
                 wait=[(QSc, P.val(QSc))], inc=[("ev", 1)])
            P.op("s", lambda s: s.activation(out=w3[:, :D], in_=s_xt[:],
                                             func=AF.Square, accum_out=s_stat[:, 1:2]),
                 wait=[(QSc, P.val(QSc))], inc=[("es", 1)])

            def stats1(v):
                v.tensor_scalar(out=s_stat[:, 2:3], in0=s_stat[:, 0:1],
                                scalar1=1.0 / D, scalar2=None, op0=ALU.mult)
                v.tensor_scalar(out=s_stat[:, 3:4], in0=s_stat[:, 1:2],
                                scalar1=1.0 / D, scalar2=None, op0=ALU.mult)
                v.tensor_mul(out=s_stat[:, 4:5], in0=s_stat[:, 2:3], in1=s_stat[:, 2:3])
                v.engine_nop()
                v.engine_nop()
                return v.tensor_sub(out=s_stat[:, 5:6], in0=s_stat[:, 3:4],
                                    in1=s_stat[:, 4:5])
            P.op("v", stats1, wait=[("es", P.val("es"))], inc=[("ev", 1)])

            def rstd1(s):
                s.activation(out=s_stat[:, 6:7], in_=s_stat[:, 5:6], func=AF.Ln,
                             bias=1e-5)
                return s.activation(out=s_stat[:, 6:7], in_=s_stat[:, 6:7],
                                    func=AF.Exp, scale=-0.5)
            P.op("s", rstd1, wait=[("ev", P.val("ev"))], inc=[("es", 1)])

            def xn_ops(v):
                v.tensor_scalar(out=s_xn[:], in0=s_xt[:], scalar1=s_stat[:, 2:3],
                                scalar2=s_stat[:, 6:7], op0=ALU.subtract, op1=ALU.mult)
                v.tensor_copy(out=s_xnb[:], in_=s_xn[:])
                return v.reduce_sum(out=s_stat[:, 8:9], in_=s_xn[:], axis=AX)
            P.op("v", xn_ops, wait=[("es", P.val("es"))], inc=[("ev", 1)])
            P.op("s", lambda s: s.activation(out=w3[:, :D], in_=s_xn[:],
                                             func=AF.Square, accum_out=s_stat[:, 9:10]),
                 wait=[("ev", P.val("ev"))], inc=[("es", 1)])

            def stats2(v):
                v.tensor_scalar(out=s_stat[:, 10:11], in0=s_stat[:, 8:9],
                                scalar1=1.0 / D, scalar2=None, op0=ALU.mult)
                v.tensor_scalar(out=s_stat[:, 11:12], in0=s_stat[:, 9:10],
                                scalar1=1.0 / D, scalar2=None, op0=ALU.mult)
                v.tensor_mul(out=s_stat[:, 12:13], in0=s_stat[:, 10:11],
                             in1=s_stat[:, 10:11])
                v.engine_nop()
                v.engine_nop()
                return v.tensor_sub(out=s_stat[:, 13:14], in0=s_stat[:, 11:12],
                                    in1=s_stat[:, 12:13])
            P.op("v", stats2, wait=[("es", P.val("es"))], inc=[("ev", 1)])

            def rstd2(s):
                s.activation(out=s_stat[:, 14:15], in_=s_stat[:, 13:14], func=AF.Ln,
                             bias=1e-5)
                return s.activation(out=s_stat[:, 14:15], in_=s_stat[:, 14:15],
                                    func=AF.Exp, scale=-0.5)
            P.op("s", rstd2, wait=[("ev", P.val("ev"))], inc=[("es", 1)])
            P.op("v", lambda v: v.tensor_scalar(out=s_xmb[:], in0=s_xn[:],
                                                scalar1=s_stat[:, 10:11],
                                                scalar2=s_stat[:, 14:15],
                                                op0=ALU.subtract, op1=ALU.mult),
                 wait=[("es", P.val("es"))], inc=[("ev", 1)])
            quant(s_xn[:], s_xq[:], D)
            prev_tile_done["es"] = P.val("es")

            for src, dstT, doff, rowlen in ((s_xmb, s_ar1, 0, 2 * NHC * TCH),
                                            (s_xq, s_ar1, NHC * TCH, 2 * NHC * TCH),
                                            (s_xnb, s_xnfm, 0, NHC * TCH)):
                for dc in range(NHC):
                    P.op("t", lambda t, src=src, dc=dc: t.matmul(
                        out=pT[:, dc * 128:(dc + 1) * 128],
                        lhsT=src[:, dc * 128:(dc + 1) * 128], rhs=s_idb[:],
                        is_transpose=True, start=True, stop=True),
                        wait=([("ev", P.val("ev"))] if dc == 0 else ()),
                        inc=([("et", 1)] if dc == NHC - 1 else ()))
                dstap = bass.AP(dstT, doff + tt * 128,
                                [[rowlen, 128], [TCH, NHC], [1, 128]])
                P.op("v", lambda v, dstap=dstap: v.tensor_copy(out=dstap, in_=pT[:, :D]),
                     wait=[("et", P.val("et"))], inc=[("ev", 1)])
            prev_tile_done["ev"] = P.val("ev")

        if ci == 0:
            dbg("xn_t0", s_xn[:], [128, D], F32)
            dbg("xq_t0", s_xq[:], [128, D], BF16)
            dbg("xmfm", s_ar1[:, :NHC * TCH], [128, NHC * TCH], BF16)
            dbg("stat_t3", s_stat[:], [128, 16], F32)
            dbg("kc", s_kc[:], [128, 4], F32)

        psum_free = {0: ("ev", P.val("ev")), 1: ("ev", P.val("ev"))}
        wf_free = {0: P.val("et"), 1: P.val("et")}
        wfb = {0: s_wf, 1: s_wg}
        for fg in range(32):
            pp = (pA, pB)[fg % 2]
            wb = wfb[fg % 2]
            for dc in range(8):
                P.op("q", lambda q, fg=fg, dc=dc, wb=wb: q.dma_start(
                    out=wb[:, dc * 128:(dc + 1) * 128],
                    in_=WINS[dc * 128:(dc + 1) * 128, fg * 128:(fg + 1) * 128]),
                    wait=([("et", wf_free[fg % 2])] if dc == 0 else ()),
                    inc=[(QSc, 16)])
            for dc in range(8):
                P.op("t", lambda t, pp=pp, dc=dc, wb=wb: t.matmul(
                    out=pp[:], lhsT=wb[:, dc * 128:(dc + 1) * 128],
                    rhs=s_ar1[:, dc * TCH:(dc + 1) * TCH],
                    start=(dc == 0), stop=(dc == 7)),
                    wait=([(QSc, P.val(QSc)), psum_free[fg % 2]] if dc == 0 else ()),
                    inc=([("et", 1)] if dc == 7 else ()))
            wf_free[fg % 2] = P.val("et")
            if fg < 16:
                dc = fg

                def conv(v, pp=pp, dc=dc, b=b):
                    toff = (b * NDC + dc) * 3
                    v.tensor_copy(out=s_xpad[:, 0:3], in_=s_ctail[:, toff:toff + 3])
                    v.tensor_scalar(out=s_xpad[:, 3:3 + TCH], in0=pp[:],
                                    scalar1=s_lnbw[:, dc:dc + 1], scalar2=None,
                                    op0=ALU.add)
                    v.tensor_scalar(out=w1[:, :TCH], in0=s_xpad[:, 0:TCH],
                                    scalar1=s_cw[:, dc * 4:dc * 4 + 1], scalar2=None,
                                    op0=ALU.mult)
                    v.scalar_tensor_tensor(out=w2[:, :TCH], in0=s_xpad[:, 1:1 + TCH],
                                           scalar=s_cw[:, dc * 4 + 1:dc * 4 + 2],
                                           in1=w1[:, :TCH], op0=ALU.mult, op1=ALU.add)
                    v.scalar_tensor_tensor(out=w1[:, :TCH], in0=s_xpad[:, 2:2 + TCH],
                                           scalar=s_cw[:, dc * 4 + 2:dc * 4 + 3],
                                           in1=w2[:, :TCH], op0=ALU.mult, op1=ALU.add)
                    ret = v.scalar_tensor_tensor(
                        out=w2[:, :TCH], in0=s_xpad[:, 3:3 + TCH],
                        scalar=s_cw[:, dc * 4 + 3:dc * 4 + 4],
                        in1=w1[:, :TCH], op0=ALU.mult, op1=ALU.add)
                    v.tensor_copy(out=s_ctail[:, toff:toff + 3],
                                  in_=s_xpad[:, TCH:TCH + 3])
                    return ret
                P.op("v", conv, wait=[("et", P.val("et"))], inc=[("ev", 1)])
                P.op("s", lambda s, dc=dc: s.activation(
                    out=s_u[:, dc * TCH:(dc + 1) * TCH], in_=w2[:, :TCH],
                    func=AF.Silu, bias=s_cb[:, dc:dc + 1]),
                    wait=[("ev", P.val("ev"))], inc=[("es", 1)])
                psum_free[fg % 2] = ("ev", P.val("ev"))
            else:
                dc = fg - 16
                P.op("s", lambda s, pp=pp, dc=dc, fg=fg: s.activation(
                    out=s_silz[:, dc * TCH:(dc + 1) * TCH], in_=pp[:],
                    func=AF.Silu, bias=s_lnbw[:, fg:fg + 1]),
                    wait=[("et", P.val("et"))], inc=[("es", 1)])
                psum_free[fg % 2] = ("es", P.val("es"))

        u_done = [("es", P.val("es")), ("ev", P.val("ev"))]
        for dc in range(NDC):
            P.op("t", lambda t, dc=dc: t.matmul(
                out=pC[:], lhsT=s_wxp[:, dc * 128:(dc + 1) * 128],
                rhs=s_u[:, dc * TCH:(dc + 1) * TCH],
                start=(dc == 0), stop=(dc == NDC - 1)),
                wait=(u_done if dc == 0 else ()),
                inc=([("et", 1)] if dc == NDC - 1 else ()))

        def xdbl_evac(v):
            v.tensor_copy(out=s_dtr[:], in_=pC[0 + 64:128, :])
            v.tensor_copy(out=s_Brow[:], in_=pC[0:16, :])
            return v.tensor_copy(out=s_Crow[:], in_=pC[32:48, :])
        P.op("v", xdbl_evac, wait=[("et", P.val("et"))], inc=[("ev", 1)])
        P.op("g", lambda g: g.dma_start(out=BCS[0:16, :], in_=s_Brow[:]),
             wait=[("ev", P.val("ev"))], inc=[(EGc, 16)])
        P.op("g", lambda g: g.dma_start(out=BCS[16:32, :], in_=s_Crow[:]),
             inc=[(EGc, 16)])

        if ci == 0:
            dbg("u_c0", s_u[:], [128, NDC * TCH], BF16)
            dbg("silz_c0", s_silz[:], [128, NDC * TCH], BF16)
            dbg("dtr_c0", s_dtr[:], [DTRK, TCH], BF16)
            dbg("brow_c0", s_Brow[:], [16, TCH], BF16)
            dbg("crow_c0", s_Crow[:], [16, TCH], BF16)

        s1_et = P.val("et")
        s1_ev = P.val("ev")
        s1_es = P.val("es")
        xtb = s_xt.ap().bitcast(BF16)
        xnb2 = s_xn.ap().bitcast(BF16)

        def wg_sl(par, dc, wdt_):
            if par == 0:
                return s_wg[:, dc * 512:dc * 512 + wdt_]
            return s_ar1[:, dc * 512:dc * 512 + wdt_]

        def wu_sl(par, dc, wdt_):
            if par == 0:
                return s_wu[:, dc * 512:dc * 512 + wdt_]
            if dc < 4:
                return xtb[:, dc * 512:dc * 512 + wdt_]
            return xnb2[:, (dc - 4) * 512:(dc - 4) * 512 + wdt_]

        wgu_free = {0: s1_et, 1: s1_et}
        pgu = {0: (pA, pB), 1: (pY[0], pY[1])}
        pf = {0: [("ev", s1_ev), ("es", s1_es)], 1: [("ev", s1_ev), ("es", s1_es)]}
        sg_free = s1_ev
        for isl, wdt_ in enumerate(ISL_W):
            i0 = isl * 512
            par = isl % 2
            for dc in range(8):
                wl = ()
                if dc == 0:
                    wl = [("et", wgu_free[par])]
                    if isl == 1:
                        wl += [("ev", s1_ev), ("es", s1_es)]
                P.op("q", lambda q, wdt_=wdt_, i0=i0, dc=dc, par=par: q.dma_start(
                    out=wg_sl(par, dc, wdt_),
                    in_=WGQT[dc * 128:(dc + 1) * 128, i0:i0 + wdt_]),
                    wait=wl, inc=[(QSc, 16)])
                P.op("q", lambda q, wdt_=wdt_, i0=i0, dc=dc, par=par: q.dma_start(
                    out=wu_sl(par, dc, wdt_),
                    in_=WUQT[dc * 128:(dc + 1) * 128, i0:i0 + wdt_]),
                    inc=[(QSc, 16)])
            for tt in range(4):
                ppg, ppu = pgu[tt % 2]
                for dc in range(8):
                    P.op("t", lambda t, dc=dc, tt=tt, wdt_=wdt_, par=par,
                         ppg=ppg, ppu=ppu: [
                        t.matmul(out=ppg[:, :wdt_],
                                 lhsT=s_ar1[:, NHC * TCH + dc * TCH + tt * 128:NHC * TCH + dc * TCH + (tt + 1) * 128],
                                 rhs=wg_sl(par, dc, wdt_),
                                 start=(dc == 0), stop=(dc == 7)),
                        t.matmul(out=ppu[:, :wdt_],
                                 lhsT=s_ar1[:, NHC * TCH + dc * TCH + tt * 128:NHC * TCH + dc * TCH + (tt + 1) * 128],
                                 rhs=wu_sl(par, dc, wdt_),
                                 start=(dc == 0), stop=(dc == 7))][-1],
                        wait=([(QSc, P.val(QSc))] + pf[tt % 2] if dc == 0 else ()),
                        inc=([("et", 1)] if dc == 7 else ()))
                P.op("s", lambda s, wdt_=wdt_, ppg=ppg: s.activation(
                    out=s_sg[:, :wdt_], in_=ppg[:, :wdt_], func=AF.Silu),
                    wait=[("et", P.val("et")), ("ev", sg_free)], inc=[("es", 1)])
                es_silu = P.val("es")
                P.op("v", lambda v, wdt_=wdt_, ppu=ppu: v.tensor_mul(
                    out=s_h[:, :wdt_], in0=s_sg[:, :wdt_], in1=ppu[:, :wdt_]),
                    wait=[("es", P.val("es"))], inc=[("ev", 1)])
                sg_free = P.val("ev")
                pf[tt % 2] = [("es", es_silu), ("ev", sg_free)]
                quant(s_h[:, :wdt_], s_hq[:, :wdt_], wdt_)
                ntb = wdt_ // 128
                for j in range(ntb):
                    P.op("t", lambda t, j=j: t.matmul(
                        out=pT[:, j * 128:(j + 1) * 128],
                        lhsT=s_hq[:, j * 128:(j + 1) * 128], rhs=s_idb[:],
                        is_transpose=True, start=True, stop=True),
                        wait=([("ev", P.val("ev"))] if j == 0 else ()),
                        inc=([("et", 1)] if j == ntb - 1 else ()))
                dstap = bass.AP(s_hqT, (isl * 4) * TCH + tt * 128,
                                [[NIC * TCH, 128], [TCH, ntb], [1, 128]])
                P.op("v", lambda v, dstap=dstap, ntb=ntb: v.tensor_copy(
                    out=dstap, in_=pT[:, :ntb * 128]),
                    wait=[("et", P.val("et"))], inc=[("ev", 1)])
            wgu_free[par] = P.val("et")
        wg_free_et = P.val("et")
        hqt_done_ev = P.val("ev")
        dn_free = {0: 0, 1: wg_free_et}
        dnb = {0: s_wdq, 1: s_ar1}
        pdn = {0: pC, 1: pY[2]}
        pc_free = {0: ("ev", hqt_done_ev), 1: ("ev", hqt_done_ev)}
        for hc in range(NHC):
            par = hc % 2
            for ic in range(NIC):
                P.op("q", lambda q, hc=hc, ic=ic, par=par: q.dma_start(
                    out=dnb[par][:, ic * 128:(ic + 1) * 128],
                    in_=WDQT[ic * 128:(ic + 1) * 128, hc * 128:(hc + 1) * 128]),
                    wait=([("et", dn_free[par])] if ic == 0 else ()),
                    inc=[(QSc, 16)])
            for ic in range(NIC):
                P.op("t", lambda t, ic=ic, par=par: t.matmul(
                    out=pdn[par][:], lhsT=dnb[par][:, ic * 128:(ic + 1) * 128],
                    rhs=s_hqT[:, ic * TCH:ic * TCH + 512],
                    start=(ic == 0), stop=(ic == NIC - 1)),
                    wait=([(QSc, P.val(QSc)), ("ev", hqt_done_ev), pc_free[par]]
                          if ic == 0 else ()),
                    inc=([("et", 1)] if ic == NIC - 1 else ()))
            P.op("v", lambda v, hc=hc, par=par: v.tensor_copy(
                out=s_asm[:, hc * TCH:(hc + 1) * TCH], in_=pdn[par][:]),
                wait=[("et", P.val("et"))], inc=[("ev", 1)])
            pc_free[par] = ("ev", P.val("ev"))
            dn_free[par] = P.val("et")

        if ci == 0:
            dbg("hqT_c0", s_hqT[:], [128, NIC * TCH], BF16)
            dbg("asm_c0", s_asm[:], [128, NHC * TCH], BF16)

        for bd in range(NBAND):
            dt_ev = []
            for j in range(BAND):
                dc = bd * BAND + j
                P.op("t", lambda t, dc=dc: t.matmul(
                    out=pC[:], lhsT=s_wdt[:, dc * 128:(dc + 1) * 128],
                    rhs=s_dtr[:], start=True, stop=True),
                    wait=[("ev", P.val("ev")), ("es", P.val("es"))],
                    inc=[("et", 1)])

                def softp_s(s, dc=dc, j=j):
                    s.activation(out=s_xpad[:, 0:TCH], in_=pC[:], func=AF.Abs,
                                 bias=s_bdt[:, dc:dc + 1])
                    s.activation(out=s_dt[:, j * TCH:(j + 1) * TCH],
                                 in_=s_xpad[:, 0:TCH], func=AF.Exp, scale=-1.0)
                    return s.activation(out=s_xpad[:, 0:TCH],
                                        in_=s_dt[:, j * TCH:(j + 1) * TCH],
                                        func=AF.Ln, bias=1.0)
                P.op("s", softp_s, wait=[("et", P.val("et")),
                                         ("ev", P.val("ev"))], inc=[("es", 1)])

                def softp_v(v, dc=dc, j=j):
                    v.tensor_scalar(out=s_dt[:, j * TCH:(j + 1) * TCH], in0=pC[:],
                                    scalar1=s_bdt[:, dc:dc + 1], scalar2=0.0,
                                    op0=ALU.add, op1=ALU.max)
                    v.tensor_add(out=s_dt[:, j * TCH:(j + 1) * TCH],
                                 in0=s_dt[:, j * TCH:(j + 1) * TCH],
                                 in1=s_xpad[:, 0:TCH])
                    return v.tensor_tensor(
                        out=s_w[:, j * TCH:(j + 1) * TCH],
                        in0=s_dt[:, j * TCH:(j + 1) * TCH],
                        in1=s_u[:, dc * TCH:(dc + 1) * TCH], op=ALU.mult)
                P.op("v", softp_v, wait=[("es", P.val("es"))], inc=[("ev", 1)])
                dt_ev.append(P.val("ev"))

            cons = {0: {"T": 0, "M": 0}, 1: {"T": 0, "M": 0}}
            bc_cons = {0: 0, 1: 0}
            pend = None

            def emit_ch_mm(j, pb2, n, nb_):
                P.op("v", lambda v, j=j, pb2=pb2, nb_=nb_: v.tensor_tensor(
                    out=s_Ch[:, pb2 * TCH:(pb2 + 1) * TCH],
                    in0=s_hs[:, j * TCH:(j + 1) * TCH],
                    in1=s_Cbc[:, nb_ * TCH:(nb_ + 1) * TCH], op=ALU.mult),
                    wait=[("et", cons[pb2]["M"])], inc=[("ev", 1)])
                P.op("t", lambda t, j=j, pb2=pb2, n=n: t.matmul(
                    out=pY[j][:], lhsT=s_idb[:],
                    rhs=s_Ch[:, pb2 * TCH:(pb2 + 1) * TCH],
                    start=(n == 0), stop=(n == 15)),
                    wait=[("ev", P.val("ev"))],
                    inc=[("et", 1)])
                cons[pb2]["M"] = P.val("et")
                if j == BAND - 1:
                    bc_cons[nb_] = P.val("ev")

            for n in range(16):
                nb_ = n % 2
                P.op("g", lambda g, n=n, nb_=nb_: g.dma_start(
                    out=s_Bbc[:, nb_ * TCH:(nb_ + 1) * TCH],
                    in_=bass.AP(BCS.tensor, n * TCH, [[0, 128], [1, TCH]])),
                    wait=[("ev", bc_cons[nb_])], inc=[(EGc, 16)])
                P.op("g", lambda g, n=n, nb_=nb_: g.dma_start(
                    out=s_Cbc[:, nb_ * TCH:(nb_ + 1) * TCH],
                    in_=bass.AP(BCS.tensor, (16 + n) * TCH, [[0, 128], [1, TCH]])),
                    inc=[(EGc, 16)])
                bc_wait = P.val(EGc)
                for j in range(BAND):
                    dc = bd * BAND + j
                    pb2 = (n * BAND + j) % 2
                    P.op("s", lambda s, j=j, dc=dc, n=n, pb2=pb2: s.activation(
                        out=s_dA[:, pb2 * TCH:(pb2 + 1) * TCH],
                        in_=s_dt[:, j * TCH:(j + 1) * TCH], func=AF.Exp,
                        scale=s_A[:, dc * NST + n:dc * NST + n + 1]),
                        wait=[("ev", max(dt_ev[j], cons[pb2]["T"]))], inc=[("es", 1)])
                    P.op("v", lambda v, j=j, pb2=pb2, nb_=nb_: v.tensor_tensor(
                        out=s_P[:, pb2 * TCH:(pb2 + 1) * TCH],
                        in0=s_w[:, j * TCH:(j + 1) * TCH],
                        in1=s_Bbc[:, nb_ * TCH:(nb_ + 1) * TCH], op=ALU.mult),
                        wait=[(EGc, bc_wait), ("ev", cons[pb2]["T"])], inc=[("ev", 1)])
                    ccol = b * (NDC * NST) + dc * NST + n
                    P.op("v", lambda v, j=j, pb2=pb2, ccol=ccol: v.tensor_tensor_scan(
                        out=s_hs[:, j * TCH:(j + 1) * TCH],
                        data0=s_dA[:, pb2 * TCH:(pb2 + 1) * TCH],
                        data1=s_P[:, pb2 * TCH:(pb2 + 1) * TCH],
                        initial=s_carry[:, ccol:ccol + 1],
                        op0=ALU.mult, op1=ALU.add),
                        wait=[("es", P.val("es"))], inc=[("ev", 1)])
                    cons[pb2]["T"] = P.val("ev")
                    if pend is not None:
                        emit_ch_mm(*pend)
                    pend = (j, pb2, n, nb_)
                coff = b * (NDC * NST) + bd * BAND * NST + n
                P.op("v", lambda v, coff=coff: v.tensor_copy(
                    out=bass.AP(s_carry, coff, [[BC * NDC * NST, 128], [NST, BAND], [1, 1]]),
                    in_=bass.AP(s_hs, TCH - 1, [[BAND * TCH, 128], [TCH, BAND], [1, 1]])),
                    inc=[("ev", 1)])
            emit_ch_mm(*pend)
            pend = None

            for j in range(BAND):
                dc = bd * BAND + j

                def gate(v, j=j, dc=dc):
                    v.scalar_tensor_tensor(out=s_dA[:, 0:TCH],
                                           in0=s_u[:, dc * TCH:(dc + 1) * TCH],
                                           scalar=s_D[:, dc:dc + 1], in1=pY[j][:],
                                           op0=ALU.mult, op1=ALU.add)
                    return v.tensor_tensor(out=s_u[:, dc * TCH:(dc + 1) * TCH],
                                           in0=s_dA[:, 0:TCH],
                                           in1=s_silz[:, dc * TCH:(dc + 1) * TCH],
                                           op=ALU.mult)
                P.op("v", gate, wait=[("et", P.val("et"))], inc=[("ev", 1)])

        if ci == 0:
            dbg("dt_c0", s_dt[:], [128, BAND * TCH], F32)
            dbg("hs_c0", s_hs[:], [128, BAND * TCH], BF16)
            dbg("yg_c0", s_u[:], [128, NDC * TCH], BF16)

        yg_done_ev = P.val("ev")
        xr_set_free = {0: wg_free_et, 1: wg_free_et}
        s4_free = dict(dn_free)
        s4_pf = {0: ("ev", yg_done_ev), 1: ("ev", yg_done_ev)}
        xq_cnt = {}

        def xr_prefetch(hcp):
            st = (hcp % 2) * 512
            for tt in range(4):
                P.op("q", lambda q, b=b, hcp=hcp, tt=tt, st=st, tok0=tok0:
                     q.dma_start(
                         out=s_xr[:, st + tt * 128:st + (tt + 1) * 128],
                         in_=X[b, tok0 + tt * 128:tok0 + (tt + 1) * 128,
                               hcp * 128:(hcp + 1) * 128]),
                     wait=([("et", xr_set_free[hcp % 2])] if tt == 0 else ()),
                     inc=[(QSc, 16)])
            xq_cnt[hcp] = P.val(QSc)

        xr_prefetch(0)
        for hc in range(NHC):
            par = hc % 2
            for dc in range(NDC):
                P.op("q", lambda q, hc=hc, dc=dc, par=par: q.dma_start(
                    out=dnb[par][:, dc * 128:(dc + 1) * 128],
                    in_=WOUTS[dc * 128:(dc + 1) * 128, hc * 128:(hc + 1) * 128]),
                    wait=([("et", s4_free[par])] if dc == 0 else ()),
                    inc=[(QSc, 16)])
            if hc + 1 < NHC:
                xr_prefetch(hc + 1)
            for dc in range(NDC):
                P.op("t", lambda t, dc=dc, par=par: t.matmul(
                    out=pdn[par][:], lhsT=dnb[par][:, dc * 128:(dc + 1) * 128],
                    rhs=s_u[:, dc * TCH:(dc + 1) * TCH],
                    start=(dc == 0), stop=False),
                    wait=([(QSc, P.val(QSc)), ("ev", yg_done_ev), s4_pf[par]]
                          if dc == 0 else ()))
            P.op("t", lambda t, hc=hc, par=par: t.matmul(
                out=pdn[par][:], lhsT=s_idb[:],
                rhs=s_xnfm[:, hc * TCH:(hc + 1) * TCH],
                start=False, stop=False))
            st4 = (hc % 2) * 512
            for tt in range(4):
                P.op("t", lambda t, tt=tt, st4=st4, par=par: t.matmul(
                    out=pdn[par][:, tt * 128:(tt + 1) * 128],
                    lhsT=s_xr[:, st4 + tt * 128:st4 + (tt + 1) * 128],
                    rhs=s_idf[:],
                    is_transpose=True, start=False, stop=(tt == 3)),
                    wait=([(QSc, xq_cnt[hc])] if tt == 0 else ()),
                    inc=([("et", 1)] if tt == 3 else ()))
            xr_set_free[hc % 2] = P.val("et")
            s4_free[par] = P.val("et")
            P.op("v", lambda v, hc=hc, par=par: v.scalar_tensor_tensor(
                out=s_sg[:], in0=pdn[par][:], scalar=1.0,
                in1=s_asm[:, hc * TCH:(hc + 1) * TCH], op0=ALU.mult, op1=ALU.add),
                wait=[("et", P.val("et")), (EGc, P.val(EGc))], inc=[("ev", 1)])
            P.op("g", lambda g, b=b, hc=hc, tok0=tok0: g.dma_start(
                out=OUT[b, hc * 128:(hc + 1) * 128, tok0:tok0 + TCH],
                in_=s_sg[:]),
                wait=[("ev", P.val("ev"))], inc=[(EGc, 16)])
            s4_pf[par] = ("ev", P.val("ev"))

        barrier()

    P.op("q", None, wait=P.waits_all())

    for s in P.cnt:
        sem_handles[s] = es.enter_context(nc.semaphore(s))

    with es, nc.Block() as block:
        def mk(name):
            def body(eng):
                for (e, fn, waits, incs) in P.items:
                    if e != name:
                        continue
                    for (sname, val) in waits:
                        if val:
                            eng.wait_ge(sem_handles[sname], val)
                    instr = fn(eng) if fn is not None else None
                    if not isinstance(instr, bass.BassInstruction):
                        instr = None
                    for (sname, by) in incs:
                        if instr is not None:
                            instr.then_inc(sem_handles[sname], by)
                            instr = None
                        else:
                            eng.sem_inc(sem_handles[sname], by)
            return body

        block.sync(mk("q"))
        block.vector(mk("v"))
        block.scalar(mk("s"))
        block.tensor(mk("t"))
        block.gpsimd(mk("g"))

    return nc, dbg_outs


def _layout_inputs(inputs, core):
    f = np.float32
    x = np.ascontiguousarray(inputs["L0_attn_x"][core * BC:(core + 1) * BC]).astype(f)
    lnw = np.ascontiguousarray(inputs["L0_attn_ln_w"].reshape(NHC, 128).T).astype(f)
    lnb = np.ascontiguousarray(inputs["L0_attn_ln_b"].reshape(NHC, 128).T).astype(f)
    alog = np.ascontiguousarray(
        inputs["L0_attn_A_log"].reshape(NDC, 128, NST).transpose(1, 0, 2)
        .reshape(128, NDC * NST)).astype(f)
    dpar = np.ascontiguousarray(inputs["L0_attn_D_param"].reshape(NDC, 128).T).astype(f)
    convw = np.ascontiguousarray(
        inputs["L0_attn_conv_w"].reshape(DI, 4).reshape(NDC, 128, 4)
        .transpose(1, 0, 2).reshape(128, NDC * 4)).astype(f)
    convb = np.ascontiguousarray(inputs["L0_attn_conv_b"].reshape(NDC, 128).T).astype(f)
    bdt = np.ascontiguousarray(inputs["L0_attn_b_dt"].reshape(NDC, 128).T).astype(f)
    wxp_in = np.asarray(inputs["L0_attn_W_xproj"], dtype=f)
    wxp = np.zeros((DI, 128), dtype=f)
    wxp[:, 0:16] = wxp_in[:, 64:80]
    wxp[:, 32:48] = wxp_in[:, 80:96]
    wxp[:, 64:128] = wxp_in[:, 0:64]
    return {
        "x_in": x,
        "lnw": lnw, "lnb": lnb, "alog": alog, "dpar": dpar,
        "convw": convw, "convb": convb, "bdt": bdt,
        "w_in": np.ascontiguousarray(inputs["L0_attn_W_in"]).astype(f),
        "w_xp": wxp,
        "w_dt": np.ascontiguousarray(inputs["L0_attn_W_dt"]).astype(f),
        "w_out": np.ascontiguousarray(inputs["L0_attn_W_out"]).astype(f),
        "w_g": np.ascontiguousarray(inputs["L0_ffn_w_gate"]).astype(f),
        "w_u": np.ascontiguousarray(inputs["L0_ffn_w_up"]).astype(f),
        "w_d": np.ascontiguousarray(inputs["L0_ffn_w_down"]).astype(f),
        "idf": np.eye(128, dtype=f),
        "idb": np.eye(128, dtype=f).astype(ml_dtypes.bfloat16),
    }


def kernel(**inputs):
    nc, _ = build_nc(debug=False)
    in_maps = [_layout_inputs(inputs, c) for c in range(NCORES)]
    res = run_bass_kernel_spmd(nc, in_maps, core_ids=list(range(NCORES)))
    parts = [np.ascontiguousarray(r["out"].transpose(0, 2, 1)) for r in res.results]
    return np.concatenate(parts, axis=0).astype(np.float32)
